# version 06 (final)
# baseline (speedup 1.0000x reference)
import numpy as np
import concourse.bass as bass
import concourse.mybir as mybir
from concourse.bass_utils import run_bass_kernel_spmd
from concourse.hw_specs import get_activation_tables

F32 = mybir.dt.float32
BF16 = mybir.dt.bfloat16
U32 = mybir.dt.uint32
AF = mybir.ActivationFunctionType
ALU = mybir.AluOpType
AX = mybir.AxisListType.X

B, S, D, NH, DH = 4, 4096, 1024, 16, 64
TOK = 2048
NTO = TOK // 128
NTF = S // 128
NSW = 4
FC = D // 128
EPS = 1e-5
NAUX = 4096
TWO_PI = 6.283185307179586
PI = 3.141592653589793
CHUNKS = {0: (0, 3), 1: (1, 2)}

C_ZERO, C_ONE, C_NEGM, C_SGN, C_EPS = 0, 1, 2, 3, 4
C_B2, C_G1, C_GB, C_DCOL = 5, 6, 7, 8
C_T1, C_T2, C_T3, C_T4 = 9, 10, 11, 12
C_BQKV = 16
C_BO = 40
C_MU2, C_R2 = 48, 64
C_MU3, C_R3 = 80, 96
C_V1, C_LV, C_NV = 112, 128, 144
C_SIG, C_GATE = 160, 176
C_W8 = 192
C_E8 = 320
C_ST1, C_ST2, C_ST3, C_ST4 = 448, 449, 450, 451
C_UNC = 512
C_BSW = 528
C_GLOB = 552
C_Y, C_T5, C_T6 = 560, 576, 592
C_UB1 = 608
C_MU4, C_V4 = 616, 632
C_ONESROW = 656
C_NEGHUGE = 13


def _setid(nc, name):
    return list(get_activation_tables(nc.m.arch).keys()).index(name)


_PHASE_LIMIT = 99
_TAP = None
_FORCE_PARITY = None
_LOADS_MASK = 127


class _StopGen(Exception):
    pass


def build_program(debug_taps=()):
    nc = bass.Bass("TRN2", target_bir_lowering=False)

    d_xfull = nc.dram_tensor("x_full", [S, D], F32, kind="ExternalInput").ap()
    d_xown = nc.dram_tensor("x_own", [TOK, D], F32, kind="ExternalInput").ap()
    d_town = nc.dram_tensor("t_own", [1, TOK], F32, kind="ExternalInput").ap()
    d_wqT = nc.dram_tensor("wqT", [D, D], F32, kind="ExternalInput").ap()
    d_wkT = nc.dram_tensor("wkT", [D, D], F32, kind="ExternalInput").ap()
    d_wvT = nc.dram_tensor("wvT", [D, D], F32, kind="ExternalInput").ap()
    d_woT = nc.dram_tensor("woT", [D, D], F32, kind="ExternalInput").ap()
    d_bqkv = nc.dram_tensor("bqkv", [128, 24], F32, kind="ExternalInput").ap()
    d_bo = nc.dram_tensor("bo_c", [128, 8], F32, kind="ExternalInput").ap()
    d_qpT = nc.dram_tensor("qprojT", [D, 128], F32, kind="ExternalInput").ap()
    d_rwT = nc.dram_tensor("routerT", [128, 48], F32, kind="ExternalInput").ap()
    d_akT = nc.dram_tensor("auxkT", [128, NAUX], F32, kind="ExternalInput").ap()
    d_av = nc.dram_tensor("auxv", [NAUX, D], F32, kind="ExternalInput").ap()
    d_rel = nc.dram_tensor("rel", [1, NAUX], F32, kind="ExternalInput").ap()
    d_uw1T = nc.dram_tensor("uw1T", [D, 256], F32, kind="ExternalInput").ap()
    d_uw2 = nc.dram_tensor("uw2c", [128, 2], F32, kind="ExternalInput").ap()
    d_ub1 = nc.dram_tensor("ub1c", [128, 2], F32, kind="ExternalInput").ap()
    d_scal = nc.dram_tensor("scal", [128, 4], F32, kind="ExternalInput").ap()
    d_out = nc.dram_tensor("out", [TOK, D], F32, kind="ExternalOutput").ap()

    d_xn1 = nc.dram_tensor("xn1_d", [S, D], BF16).ap()
    d_xn1o = nc.dram_tensor("xn1o_d", [TOK, D], BF16).ap()
    d_xn3 = nc.dram_tensor("xn3_d", [TOK, D], BF16).ap()
    d_ytT = nc.dram_tensor("ytT_d", [D, TOK], BF16).ap()
    d_aT = nc.dram_tensor("aT_d", [D, TOK], BF16).ap()
    d_ph1 = nc.dram_tensor("ph1_d", [256, TOK], BF16).ap()
    d_ccin = nc.dram_tensor("cc_in", [128, 128], F32)
    d_ccout = nc.dram_tensor("cc_out", [128, 128], F32)
    d_dbg = None
    if debug_taps:
        d_dbg = nc.dram_tensor("dbg", list(debug_taps[0]), debug_taps[1],
                               kind="ExternalOutput").ap()

    streams = {e: [] for e in ("pe", "act", "dve", "pool", "sp")}
    cap = [None]

    from contextlib import ExitStack
    with ExitStack() as _ctx:
        big8 = _ctx.enter_context(nc.sbuf_tensor("big8", [128, 16384], F32))
        kvA = _ctx.enter_context(nc.sbuf_tensor("kvA", [128, 8192], BF16))
        kvB = _ctx.enter_context(nc.sbuf_tensor("kvB", [128, 8320], BF16))
        tab = _ctx.enter_context(nc.sbuf_tensor("tab", [128, 12288], BF16))
        qta = _ctx.enter_context(nc.sbuf_tensor("qta", [128, 4096], BF16))
        pbf = _ctx.enter_context(nc.sbuf_tensor("pbf", [128, 4096], BF16))
        flex = _ctx.enter_context(nc.sbuf_tensor("flex", [128, 4096], BF16))
        scob = _ctx.enter_context(nc.sbuf_tensor("scob", [128, 4096], F32))
        wbuf = _ctx.enter_context(nc.sbuf_tensor("wbuf", [128, 2048], BF16))
        sm32 = _ctx.enter_context(nc.sbuf_tensor("sm32", [128, 800], F32))
        smbf = _ctx.enter_context(nc.sbuf_tensor("smbf", [128, 256], BF16))
        rk49 = _ctx.enter_context(nc.sbuf_tensor("rk49", [64, 4096], BF16))
        idxs = _ctx.enter_context(nc.sbuf_tensor("idxs", [128, 128], U32))
        ytmp = _ctx.enter_context(nc.sbuf_tensor("ytmp", [128, 2048], BF16))
        xtile = _ctx.enter_context(nc.sbuf_tensor("xtile", [128, 2048], F32))
        pp0 = _ctx.enter_context(nc.psum_tensor("pp0", [128, 1024], F32))
        pp1 = _ctx.enter_context(nc.psum_tensor("pp1", [128, 1024], F32))
        pp2 = _ctx.enter_context(nc.psum_tensor("pp2", [128, 1024], F32))
        ppa = _ctx.enter_context(nc.psum_tensor("ppa", [128, 1024], F32))
        _ld = _ctx.enter_context(nc.semaphore("s_ld"))
        _tr = _ctx.enter_context(nc.semaphore("s_tr"))
        _st = _ctx.enter_context(nc.semaphore("s_st"))
        _so = _ctx.enter_context(nc.semaphore("s_out"))
        _g = _ctx.enter_context(nc.semaphore("s_g"))
        _pe = _ctx.enter_context(nc.semaphore("s_pe"))
        _ac = _ctx.enter_context(nc.semaphore("s_act"))
        _dv = _ctx.enter_context(nc.semaphore("s_dve"))
        _po = _ctx.enter_context(nc.semaphore("s_pool"))
        _cc = _ctx.enter_context(nc.semaphore("s_cc"))
        block = _ctx.enter_context(nc.Block())

        class SemC:
            def __init__(self, h):
                self.h, self.n = h, 0

            def inc(self, k=1):
                self.n += k
                return self.n

        ld, tr, st, so, g = SemC(_ld), SemC(_tr), SemC(_st), SemC(_so), SemC(_g)
        pe, ac, dv, po, cc = SemC(_pe), SemC(_ac), SemC(_dv), SemC(_po), SemC(_cc)
        ALLSEMS = [ld, tr, st, so, g, pe, ac, dv, po, cc]

        xn1T = big8.bitcast(BF16)
        pq = [pp0, pp1, pp2, ppa]

        def em(e, fn):
            if _dropall[0]:
                return
            c = cap[0]
            if c is not None:
                if e in c:
                    c[e].append(fn)
                elif c.get("_drop"):
                    return
                else:
                    streams[e].append(fn)
            else:
                streams[e].append(fn)

        def W(ename, sem, v):
            if v <= 0:
                return
            em(ename, lambda e, q=sem.h, vv=v: e.wait_ge(q, vv))

        def op(ename, sem, fn):
            v = sem.inc(1)
            em(ename, lambda e, f=fn, q=sem.h: f(e).then_inc(q, 1))
            return v

        def sp_dma(dst, src, sem=None):
            sem = sem if sem is not None else ld
            v = sem.inc(16)
            em("sp", lambda e, d=dst, s=src, q=sem.h: e.dma_start(d, s).then_inc(q, 16))
            return v

        def sp_tdma(dst, src):
            v = tr.inc(16)
            em("sp", lambda e, d=dst, s=src: e.dma_start(d, s, transpose=True).then_inc(tr.h, 16))
            return v

        def g_dma(dst, src):
            v = g.inc(16)
            em("pool", lambda e, d=dst, s=src: e.dma_start(d, s).then_inc(g.h, 16))
            return v

        def col(c):
            return sm32[:, c:c + 1]

        def act_load(name):
            em("act", lambda e, i=None, nm=name: e.add_instruction(
                mybir.InstLoadActFuncSet(
                    name=nc.get_next_instruction_name(),
                    act_func_set_id=_setid(nc, nm), ins=[], outs=[])))

        _dropall = [False]

        def _ckpt(k):
            if k >= _PHASE_LIMIT:
                _dropall[0] = True

        v_bq = sp_dma(sm32[:, C_BQKV:C_BQKV + 24], d_bqkv)
        sp_dma(sm32[:, C_BO:C_BO + 8], d_bo)
        v_sc3 = sp_dma(sm32[:, C_B2:C_B2 + 3], d_scal[:, 0:3])
        v_town = sp_dma(scob[0:1, 1024:1024 + TOK], d_town)

        op("dve", dv, lambda e: e.memset(sm32[:, C_ZERO:C_ZERO + 1], 0.0))
        op("dve", dv, lambda e: e.memset(sm32[:, C_ONE:C_ONE + 1], 1.0))
        op("dve", dv, lambda e: e.memset(sm32[:, C_EPS:C_EPS + 1], EPS))
        op("dve", dv, lambda e: e.memset(sm32[:, C_NEGHUGE:C_NEGHUGE + 1], -1e30))
        v_c1 = op("dve", dv, lambda e: e.memset(smbf[:, 128:256], 1.0))
        v_c1r = op("dve", dv, lambda e: e.memset(sm32[0:1, C_ONESROW:C_ONESROW + 128], 1.0))
        W("pool", dv, v_c1)
        v_id = op("pool", po, lambda e: e.affine_select(
            smbf[:, 0:128], smbf[:, 128:256], [[-1, 128]], ALU.is_equal, 0.0,
            base=0, channel_multiplier=1))

        v_i1 = op("pool", po, lambda e: e.iota(
            col(C_T1), [[0, 1]], base=0, channel_multiplier=1,
            allow_small_or_imprecise_dtypes=True))
        W("dve", po, v_i1)
        v = op("dve", dv, lambda e: e.stream_shuffle(
            col(C_T2), col(C_T1), [(i ^ 1) for i in range(32)]))
        W("dve", dv, v)
        v_sgn = op("dve", dv, lambda e: e.tensor_tensor(
            col(C_SGN), col(C_T1), col(C_T2), ALU.subtract))

        v_ifi = op("pool", po, lambda e: e.iota(
            scob[0:1, 0:128].rearrange("a (x y z) -> a x y z", x=2, y=32),
            [[0, 2], [1, 32], [0, 2]], base=0, channel_multiplier=0,
            allow_small_or_imprecise_dtypes=True))

        act_load("natural_log_exp_and_others")
        W("act", po, v_ifi)
        v_if = op("act", ac, lambda e: e.activation(
            scob[0:1, 128:256], scob[0:1, 0:128], AF.Exp,
            bias=sm32[0:1, C_ZERO:C_ZERO + 1], scale=-float(np.log(10000.0)) / 32.0))
        act_load("trig_and_small")

        def build_tables(cos_off, sin_off, n, own):
            for j in range(n // 512):
                if own:
                    trow = scob[0:1, 1024 + j * 512:1024 + (j + 1) * 512]
                    W("pe", ld, v_town)
                else:
                    W("pool", dv, dv.n)
                    v_t = op("pool", po, lambda e, jj=j: e.iota(
                        scob[0:1, 3072:3584], [[1, 512]], base=jj * 512,
                        channel_multiplier=0,
                        allow_small_or_imprecise_dtypes=True))
                    trow = scob[0:1, 3072:3584]
                    W("pe", po, v_t)
                W("pe", ac, v_if)
                W("pe", dv, dv.n)
                v_mm = op("pe", pe, lambda e, t=trow: e.matmul(
                    pp0[:, 0:512], scob[0:1, 128:256], t, start=True, stop=True))
                W("dve", pe, v_mm)
                v1a = op("dve", dv, lambda e: e.tensor_single_scalar(
                    scob[:, 3584:4096], pp0[:, 0:512], 1.0 / TWO_PI, ALU.mult))
                W("dve", dv, v1a)
                v1b = op("dve", dv, lambda e: e.tensor_copy(
                    flex.bitcast(mybir.dt.int32)[:, 0:512], scob[:, 3584:4096]))
                W("dve", dv, v1b)
                v1c = op("dve", dv, lambda e: e.tensor_copy(
                    scob[:, 3072:3584], flex.bitcast(mybir.dt.int32)[:, 0:512]))
                W("dve", dv, v1c)
                v1 = op("dve", dv, lambda e, p=pp0: e.scalar_tensor_tensor(
                    scob[:, 3584:4096], scob[:, 3072:3584], -TWO_PI, p[:, 0:512],
                    op0=ALU.mult, op1=ALU.add))
                W("dve", dv, v1)
                v2 = op("dve", dv, lambda e: e.tensor_single_scalar(
                    pp1[:, 0:512], scob[:, 3584:4096], PI, ALU.is_gt))
                W("dve", dv, v2)
                v3 = op("dve", dv, lambda e: e.scalar_tensor_tensor(
                    scob[:, 3584:4096], pp1[:, 0:512], -TWO_PI, scob[:, 3584:4096],
                    op0=ALU.mult, op1=ALU.add))
                W("act", dv, v3)
                v4 = op("act", ac, lambda e: e.activation(
                    pp1[:, 0:512], scob[:, 3584:4096], AF.Sin,
                    bias=col(C_ZERO), scale=1.0))
                W("dve", ac, v4)
                W("dve", dv, v_sgn)
                v5 = op("dve", dv, lambda e, o=sin_off + j * 512: e.tensor_scalar(
                    tab[:, o:o + 512], pp1[:, 0:512], col(C_SGN), None, op0=ALU.mult))
                W("dve", dv, v5)
                v6 = op("dve", dv, lambda e: e.tensor_single_scalar(
                    scob[:, 3584:4096], scob[:, 3584:4096], PI / 2, ALU.add))
                W("dve", dv, v6)
                v7 = op("dve", dv, lambda e: e.tensor_single_scalar(
                    pp1[:, 0:512], scob[:, 3584:4096], PI, ALU.is_gt))
                W("dve", dv, v7)
                v8 = op("dve", dv, lambda e: e.scalar_tensor_tensor(
                    scob[:, 3584:4096], pp1[:, 0:512], -TWO_PI, scob[:, 3584:4096],
                    op0=ALU.mult, op1=ALU.add))
                W("act", dv, v8)
                v9 = op("act", ac, lambda e: e.activation(
                    pp1[:, 0:512], scob[:, 3584:4096], AF.Sin,
                    bias=col(C_ZERO), scale=1.0))
                W("dve", ac, v9)
                op("dve", dv, lambda e, o=cos_off + j * 512: e.tensor_copy(
                    tab[:, o:o + 512], pp1[:, 0:512]))

        build_tables(0, 4096, S, own=False)
        build_tables(8192, 10240, TOK, own=True)

        W("dve", ld, v_bq)
        v_bsw = op("dve", dv, lambda e: e.stream_shuffle(
            sm32[:, C_BSW:C_BSW + 24], sm32[:, C_BQKV:C_BQKV + 24],
            [(i ^ 1) for i in range(32)]))

        act_load("natural_log_exp_and_others")

        _ckpt(1)
        def ln1_tiles(src, dst, ntiles):
            v_prev_store = 0
            v_norm_hist = {}
            for t in range(ntiles):
                buf = xtile[:, (t % 2) * 1024:(t % 2) * 1024 + 1024]
                if t >= 2:
                    pass
                if (t - 2) in v_norm_hist:
                    W("sp", dv, v_norm_hist[t - 2])
                v_x = sp_dma(buf, src[t * 128:(t + 1) * 128, :])
                W("dve", ld, v_x)
                va = op("dve", dv, lambda e, b=buf: e.tensor_scalar(
                    flex[:, 0:1024], b, 1.0, None, op0=ALU.mult, op1=ALU.add,
                    accum_out=col(C_ST1)))
                W("dve", dv, va)
                vb = op("dve", dv, lambda e, b=buf: e.scalar_tensor_tensor(
                    flex[:, 1024:2048], b, 1.0, b, op0=ALU.bypass, op1=ALU.mult,
                    accum_out=col(C_ST2)))
                W("dve", dv, vb)
                vc = op("dve", dv, lambda e: e.tensor_scalar(
                    col(C_ST1), col(C_ST1), 1.0 / D, None, op0=ALU.mult))
                W("dve", dv, vc)
                vd = op("dve", dv, lambda e: e.scalar_tensor_tensor(
                    col(C_ST3), col(C_ST1), 1.0, col(C_ST1),
                    op0=ALU.bypass, op1=ALU.mult))
                W("dve", dv, vd)
                ve = op("dve", dv, lambda e: e.tensor_scalar(
                    col(C_ST2), col(C_ST2), 1.0 / D, col(C_ST3),
                    op0=ALU.mult, op1=ALU.subtract))
                W("act", dv, ve)
                vf = op("act", ac, lambda e: e.activation(
                    col(C_ST3), col(C_ST2), AF.Ln, bias=col(C_EPS), scale=1.0))
                W("act", ac, vf)
                vg = op("act", ac, lambda e: e.activation(
                    col(C_ST4), col(C_ST3), AF.Exp, bias=col(C_ZERO), scale=-0.5))
                W("dve", ac, vg)
                W("dve", st, v_prev_store)
                vn = op("dve", dv, lambda e, b=buf: e.tensor_scalar(
                    flex[:, 2048:3072], b, col(C_ST1), col(C_ST4),
                    op0=ALU.subtract, op1=ALU.mult))
                v_norm_hist[t] = vn
                W("sp", dv, vn)
                v_prev_store = sp_dma(dst[t * 128:(t + 1) * 128, :],
                                      flex[:, 2048:3072], sem=st)
            return v_prev_store

        v_xn1_done = ln1_tiles(d_xfull, d_xn1, NTF)
        v_xn1o_done = ln1_tiles(d_xown, d_xn1o, NTO)

        _ckpt(2)
        W("sp", st, v_xn1_done)
        v_xn1T = [sp_tdma(xn1T[:, f * 4096:(f + 1) * 4096],
                          d_xn1[:, f * 128:(f + 1) * 128]) for f in range(FC)]

        psum_free = {id(p): 0 for p in pq}

        def rope_evac(psum, vmm, dst, doff, cosb, sinb, toff, bc, bcs, vws=()):
            W("dve", pe, vmm)
            for s_, v_ in vws:
                W("dve", s_, v_)
            v1 = op("dve", dv, lambda e, p=psum: e.stream_shuffle(
                scob[:, 0:1024], p[:, 0:1024], [(i ^ 1) for i in range(32)]))
            v2 = op("dve", dv, lambda e, p=psum: e.scalar_tensor_tensor(
                scob[:, 1024:2048], p[:, 0:1024], col(bc),
                tab[:, cosb + toff:cosb + toff + 1024], op0=ALU.add, op1=ALU.mult))
            W("dve", dv, v1)
            v3 = op("dve", dv, lambda e: e.scalar_tensor_tensor(
                scob[:, 2048:3072], scob[:, 0:1024], col(bcs),
                tab[:, sinb + toff:sinb + toff + 1024], op0=ALU.add, op1=ALU.mult))
            W("dve", dv, v3)
            v4 = op("dve", dv, lambda e, d=dst, o=doff: e.tensor_tensor(
                d[:, o:o + 1024], scob[:, 1024:2048], scob[:, 2048:3072], ALU.add))
            return v4

        def proj_group(w_ap_fn, units, moving_fn, bias_info, out_fn, extra_pe_waits):
            v_mm = [None] * len(units)
            for fcc in range(FC):
                for ui, u in enumerate(units):
                    def _mm(e, w=w_ap_fn(fcc, u), m=moving_fn(fcc, u), p=pq[ui], i=fcc):
                        return e.matmul(p[:, 0:1024], w, m, start=(i == 0),
                                        stop=(i == FC - 1))
                    if fcc == 0:
                        for s_, v_ in extra_pe_waits(ui, u):
                            W("pe", s_, v_)
                        W("pe", dv, psum_free[id(pq[ui])])
                        W("pe", tr, v_xn1T[0] and max(v_xn1T))
                    v_mm[ui] = op("pe", pe, _mm)
            return v_mm

        v_aT_spill = [0]

        def gen_sweep(sw):
            jbase = sw * 256

            v_w = g_dma(wbuf[:, 0:2048].rearrange("p (c j) -> p c j", j=256),
                        d_wqT[:, jbase:jbase + 256].rearrange("(c p) j -> p c j", p=128))
            v_f = [0] * FC
            v_mmq = [None] * 4
            v_lastmm_fc = [0] * FC
            for fcc in range(FC):
                fb = flex[:, (fcc % 2) * 2048:(fcc % 2) * 2048 + 2048]
                if fcc >= 2:
                    W("sp", pe, v_lastmm_fc[fcc - 2])
                v_f[fcc] = sp_tdma(fb, d_xn1o[:, fcc * 128:(fcc + 1) * 128])
                for u in range(4):
                    jc, half = u // 2, u % 2
                    if fcc == 0:
                        W("pe", g, v_w)
                        W("pe", dv, psum_free[id(pq[u])])
                    W("pe", tr, v_f[fcc])
                    for q2 in range(2):
                        def _mm(e, w=wbuf[:, fcc * 256 + jc * 128:fcc * 256 + jc * 128 + 128],
                                m=fb[:, half * 1024 + q2 * 512:half * 1024 + q2 * 512 + 512],
                                p=pq[u], i=fcc, qq=q2):
                            return e.matmul(p[:, qq * 512:qq * 512 + 512], w, m,
                                            start=(i == 0), stop=(i == FC - 1))
                        v_mmq[u] = op("pe", pe, _mm)
                v_lastmm_fc[fcc] = v_mmq[3]
            for u in range(4):
                jc, half = u // 2, u % 2
                vv = rope_evac(pq[u], v_mmq[u], qta, jc * 2048 + half * 1024,
                               8192, 10240, half * 1024,
                               C_BQKV + sw * 2 + jc, C_BSW + sw * 2 + jc,
                               vws=((dv, v_bsw),))
                psum_free[id(pq[u])] = vv

            _ckpt(3.1)
            em("pool", lambda e, vv=pe.n: e.wait_ge(pe.h, vv))
            v_w = g_dma(wbuf[:, 0:2048].rearrange("p (c j) -> p c j", j=256),
                        d_wkT[:, jbase:jbase + 256].rearrange("(c p) j -> p c j", p=128))
            for grp in range(2):
                v_mmk = [None] * 4
                for fcc in range(FC):
                    for u in range(4):
                        jc = u // 2
                        tq = grp * 2 + (u % 2)
                        if fcc == 0:
                            W("pe", g, v_w)
                            W("pe", dv, psum_free[id(pq[u])])
                            W("pe", tr, max(v_xn1T))
                        for q2 in range(2):
                            def _mm(e, w=wbuf[:, fcc * 256 + jc * 128:fcc * 256 + jc * 128 + 128],
                                    m=xn1T[:, fcc * 4096 + tq * 1024 + q2 * 512:fcc * 4096 + tq * 1024 + q2 * 512 + 512],
                                    p=pq[u], i=fcc, qq=q2):
                                return e.matmul(p[:, qq * 512:qq * 512 + 512], w, m,
                                                start=(i == 0), stop=(i == FC - 1))
                            v_mmk[u] = op("pe", pe, _mm)
                for u in range(4):
                    jc = u // 2
                    tq = grp * 2 + (u % 2)
                    vv = rope_evac(pq[u], v_mmk[u], kvA, jc * 4096 + tq * 1024,
                                   0, 4096, tq * 1024,
                                   C_BQKV + 8 + sw * 2 + jc, C_BSW + 8 + sw * 2 + jc)
                    psum_free[id(pq[u])] = vv

            _ckpt(3.2)
            em("pool", lambda e, vv=pe.n: e.wait_ge(pe.h, vv))
            v_w = g_dma(wbuf[:, 0:2048].rearrange("p (c j) -> p c j", j=256),
                        d_wvT[:, jbase:jbase + 256].rearrange("(c p) j -> p c j", p=128))
            v_ones = op("dve", dv, lambda e: e.memset(kvB[:, 0:8320], 1.0))
            for grp in range(2):
                v_mmv = [None] * 4
                for fcc in range(FC):
                    for u in range(4):
                        jc = u // 2
                        tq = grp * 2 + (u % 2)
                        if fcc == 0:
                            W("pe", g, v_w)
                            W("pe", dv, psum_free[id(pq[u])])
                        for q2 in range(2):
                            def _mm(e, w=wbuf[:, fcc * 256 + jc * 128:fcc * 256 + jc * 128 + 128],
                                    m=xn1T[:, fcc * 4096 + tq * 1024 + q2 * 512:fcc * 4096 + tq * 1024 + q2 * 512 + 512],
                                    p=pq[u], i=fcc, qq=q2):
                                return e.matmul(p[:, qq * 512:qq * 512 + 512], w, m,
                                                start=(i == 0), stop=(i == FC - 1))
                            v_mmv[u] = op("pe", pe, _mm)
                for u in range(4):
                    jc = u // 2
                    tq = grp * 2 + (u % 2)
                    W("dve", pe, v_mmv[u])
                    W("dve", dv, v_ones)
                    v_ev = op("dve", dv, lambda e, p=pq[u], bc=C_BQKV + 16 + sw * 2 + jc: e.tensor_scalar(
                        flex[:, 2048:3072], p[:, 0:1024], col(bc), None, op0=ALU.add))
                    ptr = pq[u].bitcast(BF16)
                    v_tr_last = None
                    for kk in range(8):
                        kt = tq * 8 + kk
                        W("pe", dv, v_ev)
                        W("pe", po, v_id)
                        if v_tr_last is not None:
                            W("pe", dv, v_tr_last)
                        v_t = op("pe", pe, lambda e, ko=kk, pt=ptr: e.transpose(
                            pt[:, 0:128],
                            flex[:, 2048 + ko * 128:2048 + (ko + 1) * 128],
                            smbf[:, 0:128]))
                        W("dve", pe, v_t)
                        dst = kvB[:, kt * 260 + jc * 130:kt * 260 + jc * 130 + 130]
                        dstv = dst.rearrange("p (b x) -> p b x", x=65)[:, :, 0:64]
                        v_tr_last = op("dve", dv, lambda e, d=dstv, pt=ptr: e.tensor_copy(
                            d, pt[:, 0:128].rearrange("p (b x) -> p b x", x=64)))
                    psum_free[id(pq[u])] = v_tr_last

            _ckpt(3.3)
            v_init = op("dve", dv, lambda e: e.memset(sm32[0:1, C_GLOB:C_GLOB + 2], 0.0))

            def norms(src_sb, total_cols, glob_slot):
                nslots = total_cols // 512
                for sl in range(nslots):
                    v_sq = [None, None]
                    for c in range(2):
                        W("act", dv, dv.n)
                        W("act", pe, pe.n)
                        v_sq[c] = op("act", ac, lambda e, cc=c, s=src_sb, o=sl * 512: e.activation(
                            pbf[:, cc * 512:cc * 512 + 512],
                            s[:, cc * (total_cols) + o:cc * (total_cols) + o + 512],
                            AF.Square, bias=col(C_ZERO), scale=1.0))
                    for c in range(2):
                        W("pe", ac, v_sq[c])
                        W("pe", dv, v_c1)
                        v_n = op("pe", pe, lambda e, cc=c: e.matmul(
                            ppa[0:1, 0:512], smbf[:, 128:129],
                            pbf[:, cc * 512:cc * 512 + 512],
                            start=(cc == 0), stop=(cc == 1)))
                    W("dve", pe, v_n)
                    v_r = op("dve", dv, lambda e: e.reduce_max(
                        sm32[0:1, C_T1:C_T1 + 1], ppa[0:1, 0:512], axis=AX))
                    W("dve", dv, v_r)
                    v_m = op("dve", dv, lambda e, gs=glob_slot: e.tensor_tensor(
                        sm32[0:1, gs:gs + 1], sm32[0:1, gs:gs + 1],
                        sm32[0:1, C_T1:C_T1 + 1], ALU.max))
                    psum_free[id(ppa)] = max(psum_free[id(ppa)], v_r)
                    W("pe", dv, v_r)
                return v_m

            W("dve", dv, v_init)
            v_mq = norms(qta, 2048, C_GLOB)
            v_mk = norms(kvA, 4096, C_GLOB + 1)
            W("act", dv, max(v_mq, v_mk))
            v1 = op("act", ac, lambda e: e.activation(
                sm32[0:1, C_T1:C_T1 + 1], sm32[0:1, C_GLOB:C_GLOB + 1], AF.Ln,
                bias=sm32[0:1, C_ZERO:C_ZERO + 1], scale=1.0))
            v2 = op("act", ac, lambda e: e.activation(
                sm32[0:1, C_T2:C_T2 + 1], sm32[0:1, C_GLOB + 1:C_GLOB + 2], AF.Ln,
                bias=sm32[0:1, C_ZERO:C_ZERO + 1], scale=1.0))
            W("dve", ac, max(v1, v2))
            v3 = op("dve", dv, lambda e: e.tensor_tensor(
                sm32[0:1, C_T1:C_T1 + 1], sm32[0:1, C_T1:C_T1 + 1],
                sm32[0:1, C_T2:C_T2 + 1], ALU.add))
            W("act", dv, v3)
            v4 = op("act", ac, lambda e: e.activation(
                sm32[0:1, C_T2:C_T2 + 1], sm32[0:1, C_T1:C_T1 + 1], AF.Exp,
                bias=sm32[0:1, C_ZERO:C_ZERO + 1], scale=0.5))
            W("dve", ac, v4)
            v5 = op("dve", dv, lambda e: e.tensor_scalar(
                sm32[0:1, C_T1:C_T1 + 1], sm32[0:1, C_T2:C_T2 + 1],
                -1.01 / 8.0, None, op0=ALU.mult))
            W("pe", dv, v5)
            W("pe", dv, v_c1r)
            W("pe", dv, psum_free[id(ppa)])
            v_bm = op("pe", pe, lambda e: e.matmul(
                ppa[:, 0:1], sm32[0:1, C_ONESROW:C_ONESROW + 128],
                sm32[0:1, C_T1:C_T1 + 1], start=True, stop=True))
            W("dve", pe, v_bm)
            v_negm = op("dve", dv, lambda e: e.tensor_copy(
                sm32[:, C_NEGM:C_NEGM + 1], ppa[:, 0:1]))
            psum_free[id(ppa)] = v_negm

            _ckpt(3.4)
            gen_attention(sw, v_negm)

        def gen_attention(sw, v_negm):
            state0 = None
            bodies = {0: None, 1: None}
            snap = [s.n for s in ALLSEMS]
            pfree_snap = dict(psum_free)
            for par in (0, 1):
                capd = {"pe": [], "act": [], "pool": [], "_drop": par == 1}
                cap[0] = capd
                if _FORCE_PARITY == "struct" and par == 1:
                    _attn_body(sw, CHUNKS[0], v_negm, base_tweak=1)
                elif _FORCE_PARITY == "addr" and par == 1:
                    _attn_body(sw, CHUNKS[0], v_negm, kcol_tweak=128)
                elif _FORCE_PARITY == "apsz" and par == 1:
                    _attn_body(sw, CHUNKS[0], v_negm, apsz_tweak=True)
                elif _FORCE_PARITY is not None and _FORCE_PARITY not in ("struct", "addr", "apsz"):
                    _attn_body(sw, CHUNKS[_FORCE_PARITY], v_negm)
                else:
                    _attn_body(sw, CHUNKS[par], v_negm)
                cap[0] = None
                bodies[par] = capd
                endstate = [s.n for s in ALLSEMS]
                if par == 0:
                    state0 = endstate
                    pfree0 = dict(psum_free)
                    for s, v_ in zip(ALLSEMS, snap):
                        s.n = v_
                    psum_free.clear()
                    psum_free.update(pfree_snap)
                else:
                    assert endstate == state0, (
                        f"switch bodies diverge: {endstate} vs {state0}")
                    psum_free.clear()
                    psum_free.update(pfree0)
            for e in ("pe", "act", "pool"):
                assert len(bodies[0][e]) == len(bodies[1][e]), (
                    e, len(bodies[0][e]), len(bodies[1][e]))
                streams[e].append(("SW", bodies[0][e], bodies[1][e]))

        SLOT_SHAPES = (16, 32)

        def _attn_body(sw, chunks, v_negm, base_tweak=0, kcol_tweak=0, apsz_tweak=False):
            v_V_db = {0: 0, 1: 0}
            for hp in range(2):
                for lc in range(2):
                    G = chunks[lc]
                    real = 8 * G + 8
                    shape = SLOT_SHAPES[lc]
                    ndead = shape - real
                    vacc = {0: pp2, 1: ppa}
                    v_exp = {}
                    v_V_last = 0
                    for j in range(shape):
                        dead = j < ndead
                        kt = 0 if dead else (j - ndead)
                        diag = (not dead) and kt >= 8 * G
                        db = j % 2
                        N = 8 if dead else 512
                        NE = 8 if dead else 1024
                        biascol = C_NEGHUGE if dead else C_NEGM
                        if j == 0:
                            W("pe", dv, v_negm)
                            W("pe", dv, psum_free[id(pp2)])
                            W("pe", dv, psum_free[id(ppa)])
                            W("pe", dv, psum_free[id(pp0)])
                            W("pe", dv, psum_free[id(pp1)])
                        if j >= 1:
                            W("pe", ac, v_exp[j - 1])
                        v_sc = None
                        for qs in range(2):
                            qcol = hp * 2048 + lc * 1024 + qs * 512
                            kcol = hp * 4096 + kt * 128
                            for quad in range(4):
                                hl = quad // 2
                                sub = quad % 2
                                rows = slice(hl * 64, hl * 64 + 64)
                                psd = (pp0 if hl == 0 else pp1)
                                outp = psd[sub * 64:sub * 64 + 64,
                                           qs * 512:qs * 512 + N]
                                def _mm(e, o=outp,
                                        w=kvA[rows, kcol + sub * 64:kcol + sub * 64 + 64],
                                        m=qta[rows, qcol:qcol + N],
                                        tp=(hl * 64, sub * 64)):
                                    return e.matmul(o, w, m, start=True, stop=True,
                                                    tile_position=tp)
                                v_sc = op("pe", pe, _mm)
                        W("act", pe, v_sc)
                        if v_V_db[db]:
                            W("act", pe, v_V_db[db])
                        v_eA = op("act", ac, lambda e, d=db, nn=NE, bc=biascol: e.activation(
                            pbf[:, d * 2048:d * 2048 + nn], pp0[:, 0:nn],
                            AF.Exp, bias=col(bc), scale=0.125))
                        v_eB = op("act", ac, lambda e, d=db, nn=NE, bc=biascol: e.activation(
                            pbf[:, d * 2048 + 1024:d * 2048 + 1024 + nn], pp1[:, 0:nn],
                            AF.Exp, bias=col(bc), scale=0.125))
                        v_exp[j] = v_eB
                        v_gate = v_eB
                        if j >= shape - 8:
                            base = (G * 1024 - kt * 128 + base_tweak) if diag else 2048
                            W("pool", ac, v_eB)
                            v_m1 = op("pool", po, lambda e, d=db, b=base, nn=NE: e.affine_select(
                                pbf[:, d * 2048:d * 2048 + nn],
                                pbf[:, d * 2048:d * 2048 + nn],
                                [[1, nn]], ALU.is_ge, 0.0, base=b,
                                channel_multiplier=-1))
                            v_m2 = op("pool", po, lambda e, d=db, b=base, nn=NE: e.affine_select(
                                pbf[:, d * 2048 + 1024:d * 2048 + 1024 + nn],
                                pbf[:, d * 2048 + 1024:d * 2048 + 1024 + nn],
                                [[1, nn]], ALU.is_ge, 0.0, base=b,
                                channel_multiplier=-1))
                            v_gate = None
                        for hl in range(2):
                            for qs in range(2):
                                h4 = hp * 2 + hl
                                pmoff = db * 2048 + hl * 1024 + (0 if dead else qs * 512)
                                def _mmv(e, va=vacc[hl],
                                         w=kvB[:, kt * 260 + h4 * 65:kt * 260 + h4 * 65 + 65],
                                         m=pbf[:, pmoff:pmoff + N],
                                         st_=(j == 0), sp_=(j == shape - 1),
                                         q2=qs, NN=N):
                                    return e.matmul(va[0:65, q2 * 512:q2 * 512 + NN],
                                                    w, m, start=st_, stop=sp_)
                                if hl == 0 and qs == 0:
                                    if v_gate is None:
                                        W("pe", po, v_m2)
                                    else:
                                        W("pe", ac, v_gate)
                                v_V = op("pe", pe, _mmv)
                        v_V_db[db] = v_V
                        v_V_last = v_V
                    for hl in range(2):
                        va = vacc[hl]
                        h4 = hp * 2 + hl
                        W("dve", pe, v_V_last)
                        v_rc = op("dve", dv, lambda e, p=va: e.reciprocal(
                            flex[64:65, 3072:4096], p[64:65, 0:1024]))
                        W("pe", dv, v_rc)
                        for qs in range(2):
                            v_zb = op("pe", pe, lambda e, q2=qs: e.matmul(
                                pp0[0:64, q2 * 512:q2 * 512 + 512],
                                smbf[64:65, 192:256],
                                flex[64:65, 3072 + q2 * 512:3072 + q2 * 512 + 512],
                                start=True, stop=True, tile_position=(64, 0)))
                        W("dve", pe, v_zb)
                        v_cp = op("dve", dv, lambda e: e.tensor_copy(
                            scob[0:64, 1024:2048], pp0[0:64, 0:1024]))
                        W("dve", dv, v_cp)
                        v_nm = op("dve", dv, lambda e, p=va, hh=hl, ll=lc: e.tensor_tensor(
                            ytmp[(hh * 64):(hh * 64) + 64, ll * 1024:ll * 1024 + 1024],
                            p[0:64, 0:1024], scob[0:64, 1024:2048], ALU.mult))
                        psum_free[id(va)] = v_nm
                        psum_free[id(pp0)] = max(psum_free[id(pp0)], v_nm)
                W("sp", dv, dv.n)
                v_sp = sp_dma(d_aT[(sw * 2 + hp) * 128:(sw * 2 + hp) * 128 + 128, :],
                              ytmp[:, 0:2048], sem=st)
                v_aT_spill[0] = v_sp
                W("dve", st, v_sp)

        v_or = op("dve", dv, lambda e: e.memset(smbf[64:65, 192:256], 1.0))

        _ckpt(3)
        for sw in range(NSW):
            gen_sweep(sw)
            _ckpt(3.5)

        _ckpt(4)
        v_wo = g_dma(tab[:, 0:8192].rearrange("p (c j) -> p c j", j=1024),
                     d_woT.rearrange("(c p) j -> p c j", p=128))
        W("sp", st, v_aT_spill[0])
        v_aT = []
        for cchunk in range(8):
            dstT = kvA if cchunk < 4 else kvB
            off = (cchunk % 4) * 2048
            v_aT.append(sp_dma(dstT[:, off:off + 2048],
                               d_aT[cchunk * 128:(cchunk + 1) * 128, :]))
        v_yst = 0
        for jc in range(8):
            for half in range(2):
                ps = pq[(jc * 2 + half) % 2]
                v_mm = None
                for fcc in range(FC):
                    mvT = kvA if fcc < 4 else kvB
                    moff = (fcc % 4) * 2048 + half * 1024
                    if fcc == 0:
                        W("pe", g, v_wo)
                        W("pe", ld, max(v_aT))
                        W("pe", dv, psum_free[id(ps)])
                    for q2 in range(2):
                        def _mm(e, w=tab[:, fcc * 1024 + jc * 128:fcc * 1024 + jc * 128 + 128],
                                m=mvT[:, moff + q2 * 512:moff + q2 * 512 + 512], p=ps, i=fcc, qq=q2):
                            return e.matmul(p[:, qq * 512:qq * 512 + 512], w, m,
                                            start=(i == 0), stop=(i == FC - 1))
                        v_mm = op("pe", pe, _mm)
                W("dve", pe, v_mm)
                W("dve", st, v_yst)
                v_ev = op("dve", dv, lambda e, p=ps, bc=C_BO + jc, hh=half: e.tensor_scalar(
                    ytmp[:, hh * 1024:hh * 1024 + 1024], p[:, 0:1024],
                    col(bc), None, op0=ALU.add))
                psum_free[id(ps)] = v_ev
                W("sp", dv, v_ev)
                v_yst = sp_dma(d_ytT[jc * 128:(jc + 1) * 128, half * 1024:half * 1024 + 1024],
                               ytmp[:, half * 1024:half * 1024 + 1024], sem=st)

        _ckpt(5)
        W("sp", st, v_yst)
        v_x1 = [0] * NTO
        v_yload_hist = {}
        for t in range(NTO):
            yb = ytmp[:, (t % 2) * 1024:(t % 2) * 1024 + 1024]
            if (t - 2) in v_yload_hist:
                W("sp", dv, v_yload_hist[t - 2])
            v_y = sp_tdma(yb, d_ytT[:, t * 128:(t + 1) * 128])
            xb = xtile[:, (t % 2) * 1024:(t % 2) * 1024 + 1024]
            if t >= 2:
                W("sp", dv, v_x1[t - 2])
            v_x = sp_dma(xb, d_xown[t * 128:(t + 1) * 128, :])
            W("dve", tr, v_y)
            va = op("dve", dv, lambda e, b=yb: e.tensor_scalar(
                flex.bitcast(F32)[:, 0:1024] if False else scob[:, 0:1024],
                b, 1.0, None, op0=ALU.mult, op1=ALU.add, accum_out=col(C_ST1)))
            W("dve", dv, va)
            vb = op("dve", dv, lambda e, b=yb: e.scalar_tensor_tensor(
                scob[:, 1024:2048], b, 1.0, b, op0=ALU.bypass, op1=ALU.mult,
                accum_out=col(C_ST2)))
            W("dve", dv, vb)
            vc = op("dve", dv, lambda e, t_=t: e.tensor_scalar(
                col(C_MU2 + t_), col(C_ST1), 1.0 / D, None, op0=ALU.mult))
            W("dve", dv, vc)
            vd = op("dve", dv, lambda e, t_=t: e.scalar_tensor_tensor(
                col(C_ST3), col(C_MU2 + t_), 1.0, col(C_MU2 + t_),
                op0=ALU.bypass, op1=ALU.mult))
            W("dve", dv, vd)
            ve = op("dve", dv, lambda e: e.tensor_scalar(
                col(C_ST2), col(C_ST2), 1.0 / D, col(C_ST3),
                op0=ALU.mult, op1=ALU.subtract))
            W("act", dv, ve)
            vf = op("act", ac, lambda e: e.activation(
                col(C_ST3), col(C_ST2), AF.Ln, bias=col(C_EPS), scale=1.0))
            W("act", ac, vf)
            vg = op("act", ac, lambda e, t_=t: e.activation(
                col(C_R2 + t_), col(C_ST3), AF.Exp, bias=col(C_ZERO), scale=-0.5))
            W("dve", ac, vg)
            vh = op("dve", dv, lambda e, b=yb, t_=t: e.tensor_scalar(
                scob[:, 0:1024], b, col(C_MU2 + t_), col(C_R2 + t_),
                op0=ALU.subtract, op1=ALU.mult))
            v_yload_hist[t] = vh
            W("dve", ld, v_x)
            W("dve", dv, vh)
            v_x1[t] = op("dve", dv, lambda e, b=xb, t_=t: e.tensor_tensor(
                big8[:, t_ * 1024:t_ * 1024 + 1024], b, scob[:, 0:1024], ALU.add))

        _ckpt(6)
        v_xn3st = 0
        for t in range(NTO):
            xb = big8[:, t * 1024:t * 1024 + 1024]
            W("dve", dv, v_x1[t])
            va = op("dve", dv, lambda e, b=xb: e.tensor_scalar(
                scob[:, 0:1024], b, 1.0, None, op0=ALU.mult, op1=ALU.add, accum_out=col(C_ST1)))
            W("dve", dv, va)
            vb = op("dve", dv, lambda e, b=xb: e.scalar_tensor_tensor(
                scob[:, 1024:2048], b, 1.0, b, op0=ALU.bypass, op1=ALU.mult,
                accum_out=col(C_ST2)))
            W("dve", dv, vb)
            vc = op("dve", dv, lambda e, t_=t: e.tensor_scalar(
                col(C_MU3 + t_), col(C_ST1), 1.0 / D, None, op0=ALU.mult))
            W("dve", dv, vc)
            vd = op("dve", dv, lambda e, t_=t: e.scalar_tensor_tensor(
                col(C_ST3), col(C_MU3 + t_), 1.0, col(C_MU3 + t_),
                op0=ALU.bypass, op1=ALU.mult))
            W("dve", dv, vd)
            ve = op("dve", dv, lambda e, t_=t: e.tensor_scalar(
                col(C_V1 + t_), col(C_ST2), 1.0 / D, col(C_ST3),
                op0=ALU.mult, op1=ALU.subtract))
            W("act", dv, ve)
            vf = op("act", ac, lambda e, t_=t: e.activation(
                col(C_ST3), col(C_V1 + t_), AF.Ln, bias=col(C_EPS), scale=1.0))
            W("act", ac, vf)
            vg = op("act", ac, lambda e, t_=t: e.activation(
                col(C_R3 + t_), col(C_ST3), AF.Exp, bias=col(C_ZERO), scale=-0.5))
            W("dve", ac, vg)
            W("dve", st, v_xn3st)
            vn = op("dve", dv, lambda e, b=xb, t_=t: e.tensor_scalar(
                flex[:, 2048:3072], b, col(C_MU3 + t_), col(C_R3 + t_),
                op0=ALU.subtract, op1=ALU.mult))
            W("sp", dv, vn)
            v_xn3st = sp_dma(d_xn3[t * 128:(t + 1) * 128, :], flex[:, 2048:3072], sem=st)
            W("dve", ac, vg)
            v1_ = op("dve", dv, lambda e, t_=t: e.scalar_tensor_tensor(
                col(C_ST4), col(C_R3 + t_), 1.0, col(C_R3 + t_),
                op0=ALU.bypass, op1=ALU.mult))
            W("dve", dv, v1_)
            v2_ = op("dve", dv, lambda e, t_=t: e.scalar_tensor_tensor(
                col(C_NV + t_), col(C_V1 + t_), 1.0, col(C_ST4),
                op0=ALU.bypass, op1=ALU.mult))
            W("act", dv, v2_)
            op("act", ac, lambda e, t_=t: e.activation(
                col(C_LV + t_), col(C_NV + t_), AF.Ln, bias=col(C_ONE), scale=1.0))

        W("dve", ac, ac.n)
        v_ls = op("dve", dv, lambda e: e.reduce_sum(
            col(C_T1), sm32[:, C_LV:C_LV + 16], axis=AX))
        W("pe", dv, v_ls)
        v_gs = op("pe", pe, lambda e: e.matmul(
            ppa[0:1, 0:1], col(C_T1).bitcast(BF16)[:, 0:1] if False else col(C_T1),
            col(C_ONE), start=True, stop=True))
        W("dve", pe, v_gs)
        v_gc = op("dve", dv, lambda e: e.tensor_copy(sm32[0:1, C_GLOB + 2:C_GLOB + 3],
                                                     ppa[0:1, 0:1]))
        psum_free[id(ppa)] = v_gc
        v_z = op("dve", dv, lambda e: e.memset(scob[:, 3072:3200], 0.0))
        W("sp", dv, v_z)
        v_ci = sp_dma(d_ccin[:, :].ap() if False else d_ccin.ap(), scob[:, 3072:3200], sem=st)
        W("sp", dv, v_gc)
        W("sp", st, v_ci)
        v_ci2 = sp_dma(d_ccin.ap()[0:1, 0:1], sm32[0:1, C_GLOB + 2:C_GLOB + 3], sem=st)
        W("pool", st, v_ci2)
        v_cc = cc.inc(1)
        em("pool", lambda e: e.collective_compute(
            "AllReduce", ALU.add, replica_groups=[list(range(8))],
            ins=[d_ccin.ap().opt()], outs=[d_ccout.ap().opt()]).then_inc(cc.h, 1))
        W("sp", cc, v_cc)
        v_co = sp_dma(sm32[0:1, C_GLOB + 3:C_GLOB + 4], d_ccout.ap()[0:1, 0:1])
        W("dve", ld, v_co)
        v_d1 = op("dve", dv, lambda e: e.tensor_scalar(
            sm32[0:1, C_T1:C_T1 + 1], sm32[0:1, C_GLOB + 3:C_GLOB + 4],
            1.0 / 16384.0, 1e-6, op0=ALU.mult, op1=ALU.add))
        W("dve", dv, v_d1)
        v_d2 = op("dve", dv, lambda e: e.reciprocal(
            sm32[0:1, C_T2:C_T2 + 1], sm32[0:1, C_T1:C_T1 + 1]))
        W("pe", dv, v_d2)
        W("pe", dv, psum_free[id(ppa)])
        v_bd = op("pe", pe, lambda e: e.matmul(
            ppa[:, 0:1], sm32[0:1, C_ONESROW:C_ONESROW + 128],
            sm32[0:1, C_T2:C_T2 + 1], start=True, stop=True))
        W("dve", pe, v_bd)
        v_dcol = op("dve", dv, lambda e: e.tensor_copy(col(C_DCOL), ppa[:, 0:1]))
        psum_free[id(ppa)] = v_dcol

        W("sp", st, v_xn3st)
        W("sp", pe, pe.n)
        v_xn3T = []
        for f in range(FC):
            dstT = kvA if f < 4 else kvB
            off = (f % 4) * 2048
            v_xn3T.append(sp_tdma(dstT[:, off:off + 2048],
                                  d_xn3[:, f * 128:(f + 1) * 128]))

        _ckpt(7)
        LM = _LOADS_MASK
        v_qp = g_dma(flex[:, 0:1024].rearrange("p (c j) -> p c j", j=128),
                     d_qpT.rearrange("(c p) j -> p c j", p=128)) if LM & 1 else 0
        v_rw = g_dma(flex[:, 1024:1072], d_rwT) if LM & 2 else 0
        v_ak = g_dma(tab[:, 0:4096], d_akT) if LM & 4 else 0
        v_u1 = g_dma(wbuf[:, 0:2048].rearrange("p (c j) -> p c j", j=256),
                     d_uw1T.rearrange("(c p) j -> p c j", p=128)) if LM & 8 else 0
        v_u2 = g_dma(smbf[:, 130:132], d_uw2) if LM & 16 else 0
        v_ub = sp_dma(sm32[:, C_UB1:C_UB1 + 2], d_ub1) if LM & 32 else 0
        v_rl = g_dma(rk49[48:49, 0:4096], d_rel) if LM & 64 else 0

        _ckpt(7.02)
        v_qev = [0, 0]
        for half in range(2):
            ps = pq[half]
            v_mm = None
            for fcc in range(FC):
                mvT = kvA if fcc < 4 else kvB
                moff = (fcc % 4) * 2048 + half * 1024
                if fcc == 0:
                    W("pe", g, v_qp)
                    W("pe", tr, max(v_xn3T))
                    W("pe", dv, psum_free[id(ps)])
                for q2 in range(2):
                    def _mm(e, w=flex[:, fcc * 128:fcc * 128 + 128],
                            m=mvT[:, moff + q2 * 512:moff + q2 * 512 + 512], p=ps, i=fcc, qq=q2):
                        return e.matmul(p[:, qq * 512:qq * 512 + 512], w, m,
                                        start=(i == 0), stop=(i == FC - 1))
                    v_mm = op("pe", pe, _mm)
            W("act", pe, v_mm)
            v_qev[half] = op("act", ac, lambda e, p=ps, hh=half: e.activation(
                pbf[:, hh * 1024:hh * 1024 + 1024], p[:, 0:1024], AF.Copy,
                bias=0.0, scale=1.0))
            psum_free[id(ps)] = 0
        _ckpt(7.05)
        v_ms = op("dve", dv, lambda e: e.memset(pbf[32:64, 2048:4096], 1.0))
        v_rqe = 0
        for half in range(2):
            W("pe", ac, v_qev[half])
            W("pe", dv, v_rqe)
            for q2 in range(2):
                v_mm = op("pe", pe, lambda e, hh=half, qq=q2: e.matmul(
                    pp2[0:48, qq * 512:qq * 512 + 512],
                    flex[:, 1024:1072],
                    pbf[:, hh * 1024 + qq * 512:hh * 1024 + qq * 512 + 512],
                    start=True, stop=True))
            W("dve", pe, v_mm)
            W("dve", dv, v_ms)
            v_rqe = op("dve", dv, lambda e, hh=half: e.tensor_scalar(
                pbf[0:48, 2048 + hh * 1024:2048 + hh * 1024 + 1024],
                pp2[0:48, 0:1024], 1.0 / float(np.sqrt(48.0)), None, op0=ALU.mult))
        _ckpt(7.1)
        for i8 in range(8):
            W("pe", g, v_ak)
            W("pe", dv, dv.n if i8 == 0 else 0)
            v_mm = op("pe", pe, lambda e, i_=i8: e.matmul(
                ppa[0:48, 0:512], flex[:, 1024:1072],
                tab[:, i_ * 512:(i_ + 1) * 512], start=True, stop=True))
            W("dve", pe, v_mm)
            v_e = op("dve", dv, lambda e, i_=i8: e.tensor_copy(
                rk49[0:48, i_ * 512:(i_ + 1) * 512], ppa[0:48, 0:512]))
            W("pe", dv, v_e)
            psum_free[id(ppa)] = v_e
        _ckpt(7.15)
        v_phst = 0
        for jc in range(2):
            for half in range(2):
                ps = pq[half]
                v_mm = None
                for fcc in range(FC):
                    mvT = kvA if fcc < 4 else kvB
                    moff = (fcc % 4) * 2048 + half * 1024
                    if fcc == 0:
                        W("pe", g, v_u1)
                        W("pe", ac, v_qev[half])
                        W("pe", dv, dv.n)
                    for q2 in range(2):
                        def _mm(e, w=wbuf[:, fcc * 256 + jc * 128:fcc * 256 + jc * 128 + 128],
                                m=mvT[:, moff + q2 * 512:moff + q2 * 512 + 512], p=ps, i=fcc, qq=q2):
                            return e.matmul(p[:, qq * 512:qq * 512 + 512], w, m,
                                            start=(i == 0), stop=(i == FC - 1))
                        v_mm = op("pe", pe, _mm)
                W("dve", pe, v_mm)
                W("dve", st, v_phst)
                v_e = op("dve", dv, lambda e, p=ps: e.tensor_copy(
                    flex[:, 2048:3072], p[:, 0:1024]))
                W("sp", dv, v_e)
                v_phst = sp_dma(d_ph1[jc * 128:(jc + 1) * 128,
                                      half * 1024:half * 1024 + 1024],
                                flex[:, 2048:3072], sem=st)

        _ckpt(8)
        v_agg = [0] * NTO
        v_sc_evac = {}
        for t in range(NTO):
            for c4 in range(4):
                ps = pq[c4 % 3]
                if (t, c4 - 3) in v_sc_evac:
                    W("pe", ac, v_sc_evac[(t, c4 - 3)])
                elif t >= 1 and (t - 1, c4) in v_sc_evac:
                    W("pe", ac, v_sc_evac[(t - 1, c4)])
                if t == 0 and c4 == 0:
                    W("pe", g, v_rl)
                    W("pe", dv, dv.n)
                v_mm = None
                for half in range(2):
                    def _mm(e, p=ps,
                            w=pbf[0:49, 2048 + t * 128:2048 + t * 128 + 128],
                            m=rk49[0:49, c4 * 1024 + half * 512:c4 * 1024 + half * 512 + 512],
                            hh=half):
                        return e.matmul(p[:, hh * 512:hh * 512 + 512], w, m,
                                        start=True, stop=True)
                    v_mm = op("pe", pe, _mm)
                W("act", pe, v_mm)
                if t >= 1:
                    W("act", dv, v_agg[t - 1])
                v_sc_evac[(t, c4)] = op("act", ac, lambda e, p=ps, c_=c4: e.activation(
                    scob[:, c_ * 1024:c_ * 1024 + 1024], p[:, 0:1024], AF.Copy,
                    bias=0.0, scale=1.0))
            if _PHASE_LIMIT <= 7.2:
                continue
            W("dve", ac, v_sc_evac[(t, 3)])
            v_m8 = op("dve", dv, lambda e, t_=t: e.max(
                out=sm32[:, C_E8 + t_ * 8:C_E8 + t_ * 8 + 8], in_=scob[:, 0:4096]))
            W("dve", dv, v_m8)
            v_i8 = op("dve", dv, lambda e, t_=t: e.max_index(
                out=idxs[:, t_ * 8:t_ * 8 + 8],
                in_max=sm32[:, C_E8 + t_ * 8:C_E8 + t_ * 8 + 8],
                in_values=scob[:, 0:4096]))
            if _PHASE_LIMIT <= 7.4:
                continue
            W("pool", dv, v_i8)
            if t >= 1:
                W("pool", dv, v_agg[t - 1])
            v_g8 = 0
            for k8 in range(8):
                v_g8 = g.inc(16)
                em("pool", lambda e, t_=t, k_=k8: e.indirect_dma_start(
                    tab[:, 4096 + k_ * 1024:4096 + (k_ + 1) * 1024], None,
                    d_av, bass.IndirectOffsetOnAxis(idxs[:, t_ * 8 + k_:t_ * 8 + k_ + 1], 0),
                ).then_inc(g.h, 16))
            if _PHASE_LIMIT <= 7.6:
                continue
            W("dve", dv, v_m8)
            v_n1 = op("dve", dv, lambda e, t_=t: e.tensor_scalar(
                col(C_ST1), sm32[:, C_E8 + t_ * 8:C_E8 + t_ * 8 + 1], -1.0, None,
                op0=ALU.mult))
            W("act", dv, v_n1)
            v_e8 = op("act", ac, lambda e, t_=t: e.activation(
                sm32[:, C_W8 + t_ * 8:C_W8 + t_ * 8 + 8],
                sm32[:, C_E8 + t_ * 8:C_E8 + t_ * 8 + 8], AF.Exp,
                bias=col(C_ST1), scale=1.0))
            W("dve", ac, v_e8)
            v_z8 = op("dve", dv, lambda e, t_=t: e.reduce_sum(
                col(C_ST2), sm32[:, C_W8 + t_ * 8:C_W8 + t_ * 8 + 8], axis=AX))
            W("dve", dv, v_z8)
            v_zr = op("dve", dv, lambda e: e.reciprocal(col(C_ST3), col(C_ST2)))
            W("dve", dv, v_zr)
            v_w8 = op("dve", dv, lambda e, t_=t: e.tensor_scalar(
                sm32[:, C_W8 + t_ * 8:C_W8 + t_ * 8 + 8],
                sm32[:, C_W8 + t_ * 8:C_W8 + t_ * 8 + 8], col(C_ST3), None,
                op0=ALU.mult))
            W("dve", g, v_g8)
            W("dve", dv, v_w8)
            vk = op("dve", dv, lambda e, t_=t: e.tensor_scalar(
                scob[:, 0:1024], tab[:, 4096:5120],
                sm32[:, C_W8 + t_ * 8:C_W8 + t_ * 8 + 1], None, op0=ALU.mult))
            for k8 in range(1, 8):
                W("dve", dv, vk)
                vk = op("dve", dv, lambda e, t_=t, k_=k8: e.scalar_tensor_tensor(
                    scob[:, 0:1024], tab[:, 4096 + k_ * 1024:4096 + (k_ + 1) * 1024],
                    sm32[:, C_W8 + t_ * 8 + k_:C_W8 + t_ * 8 + k_ + 1],
                    scob[:, 0:1024], op0=ALU.mult, op1=ALU.add))
            W("dve", dv, vk)
            dstT = kvA if t < 8 else kvB
            v_agg[t] = op("dve", dv, lambda e, t_=t, dT=dstT: e.tensor_copy(
                dT[:, (t_ % 8) * 1024:(t_ % 8) * 1024 + 1024], scob[:, 0:1024]))

        _ckpt(9)
        act_load("gelu_and_others")
        v_h1 = [0] * 4
        for jc in range(2):
            for half in range(2):
                yb = ytmp[:, ((jc * 2 + half) % 2) * 1024:((jc * 2 + half) % 2) * 1024 + 1024]
                if jc * 2 + half >= 2:
                    W("sp", ac, v_h1[jc * 2 + half - 2])
                v_l = sp_dma(yb, d_ph1[jc * 128:(jc + 1) * 128,
                                       half * 1024:half * 1024 + 1024])
                W("act", ld, v_l)
                W("act", ld, v_ub)
                v_h1[jc * 2 + half] = op("act", ac, lambda e, b=yb, j=jc, h=half: e.activation(
                    qta[:, j * 2048 + h * 1024:j * 2048 + h * 1024 + 1024], b,
                    AF.Gelu, bias=col(C_UB1 + j), scale=1.0))
        W("pe", dv, psum_free[id(ppa)])
        v_lmm = 0
        for t in range(NTO):
            for jc in range(2):
                def _mm(e, t_=t, j=jc):
                    return e.matmul(ppa[:, t_:t_ + 1],
                                    qta[:, j * 2048 + t_ * 128:j * 2048 + t_ * 128 + 128],
                                    smbf[:, 130 + j:131 + j],
                                    start=(j == 0), stop=(j == 1))
                if t == 0 and jc == 0:
                    W("pe", ac, max(v_h1))
                    W("pe", g, v_u2)
                v_lmm = op("pe", pe, _mm)
        act_load("sigmoid_and_others")
        W("act", pe, v_lmm)
        W("act", ld, v_sc3)
        v_sig = op("act", ac, lambda e: e.activation(
            sm32[:, C_SIG:C_SIG + 16], ppa[:, 0:16], AF.Sigmoid,
            bias=col(C_B2), scale=1.0))
        W("dve", dv, v_dcol)
        W("dve", ac, v_sig)
        v1 = op("dve", dv, lambda e: e.tensor_scalar(
            sm32[:, C_NV:C_NV + 16], sm32[:, C_LV:C_LV + 16], col(C_DCOL), 0.5,
            op0=ALU.mult, op1=ALU.mult))
        W("dve", dv, v1)
        v2 = op("dve", dv, lambda e: e.scalar_tensor_tensor(
            sm32[:, C_UNC:C_UNC + 16], sm32[:, C_SIG:C_SIG + 16], 2.5,
            sm32[:, C_NV:C_NV + 16], op0=ALU.mult, op1=ALU.add))
        W("dve", dv, v2)
        v3 = op("dve", dv, lambda e: e.tensor_scalar(
            sm32[:, C_UNC:C_UNC + 16], sm32[:, C_UNC:C_UNC + 16], 0.0, 5.0,
            op0=ALU.max, op1=ALU.min))
        W("act", dv, v3)
        v_gate = op("act", ac, lambda e: e.activation(
            sm32[:, C_GATE:C_GATE + 16], sm32[:, C_UNC:C_UNC + 16], AF.Sigmoid,
            bias=col(C_GB), scale=col(C_G1)))

        _ckpt(10)
        v_store = 0
        for t in range(NTO):
            xb = big8[:, t * 1024:t * 1024 + 1024]
            W("dve", ac, v_gate)
            W("dve", so, v_store)
            va = op("dve", dv, lambda e, b=xb, t_=t: e.tensor_scalar(
                scob[:, 1024:2048], b, col(C_MU3 + t_), col(C_R3 + t_),
                op0=ALU.subtract, op1=ALU.mult))
            W("dve", dv, va)
            dstT = kvA if t < 8 else kvB
            vf_ = op("dve", dv, lambda e, t_=t, dT=dstT: e.scalar_tensor_tensor(
                scob[:, 1024:2048], dT[:, (t_ % 8) * 1024:(t_ % 8) * 1024 + 1024],
                col(C_GATE + t_), scob[:, 1024:2048], op0=ALU.mult, op1=ALU.add))
            W("dve", dv, vf_)
            vb = op("dve", dv, lambda e: e.tensor_scalar(
                scob[:, 2048:3072], scob[:, 1024:2048], 1.0, None, op0=ALU.mult,
                op1=ALU.add, accum_out=col(C_ST1)))
            W("dve", dv, vb)
            vc = op("dve", dv, lambda e: e.scalar_tensor_tensor(
                scob[:, 3072:4096], scob[:, 1024:2048], 1.0, scob[:, 1024:2048],
                op0=ALU.bypass, op1=ALU.mult, accum_out=col(C_ST2)))
            W("dve", dv, vc)
            vd = op("dve", dv, lambda e: e.tensor_scalar(
                col(C_ST1), col(C_ST1), 1.0 / D, None, op0=ALU.mult))
            W("dve", dv, vd)
            ve = op("dve", dv, lambda e: e.scalar_tensor_tensor(
                col(C_ST3), col(C_ST1), 1.0, col(C_ST1), op0=ALU.bypass, op1=ALU.mult))
            W("dve", dv, ve)
            vv = op("dve", dv, lambda e: e.tensor_scalar(
                col(C_ST2), col(C_ST2), 1.0 / D, col(C_ST3),
                op0=ALU.mult, op1=ALU.subtract))
            W("dve", dv, vv)
            va_ = op("dve", dv, lambda e: e.tensor_scalar(
                col(C_ST2), col(C_ST2), EPS, None, op0=ALU.add))
            W("dve", dv, va_)
            vy = op("dve", dv, lambda e: e.memset(col(C_Y), 1.0))
            for _it in range(6):
                W("dve", dv, vy)
                t1 = op("dve", dv, lambda e: e.scalar_tensor_tensor(
                    col(C_T5), col(C_Y), 1.0, col(C_Y), op0=ALU.bypass, op1=ALU.mult))
                W("dve", dv, t1)
                t2 = op("dve", dv, lambda e: e.scalar_tensor_tensor(
                    col(C_T5), col(C_T5), 1.0, col(C_ST2), op0=ALU.bypass, op1=ALU.mult))
                W("dve", dv, t2)
                t3 = op("dve", dv, lambda e: e.tensor_scalar(
                    col(C_T5), col(C_T5), -0.5, 1.5, op0=ALU.mult, op1=ALU.add))
                W("dve", dv, t3)
                vy = op("dve", dv, lambda e: e.scalar_tensor_tensor(
                    col(C_Y), col(C_Y), 1.0, col(C_T5), op0=ALU.bypass, op1=ALU.mult))
            W("dve", dv, vy)
            vn = op("dve", dv, lambda e: e.tensor_scalar(
                scob[:, 1024:2048], scob[:, 1024:2048], col(C_ST1), col(C_Y),
                op0=ALU.subtract, op1=ALU.mult))
            W("dve", dv, vn)
            vx2 = op("dve", dv, lambda e, b=xb: e.tensor_tensor(
                scob[:, 2048:3072], b, scob[:, 1024:2048], ALU.add))
            W("sp", dv, vx2)
            if _TAP == "x1":
                v_store = sp_dma(d_out[t * 128:(t + 1) * 128, :],
                                 big8[:, t * 1024:t * 1024 + 1024], sem=so)
            elif _TAP == "h":
                v_store = sp_dma(d_out[t * 128:(t + 1) * 128, :],
                                 scob[:, 1024:2048], sem=so)
            else:
                v_store = sp_dma(d_out[t * 128:(t + 1) * 128, :], scob[:, 2048:3072],
                                 sem=so)

        W("sp", so, so.n)

        def runner(items, e, par_val):
          with nc.allow_low_precision(reason="bf16 operand stores, fp32 accum"):
            for it in items:
                if isinstance(it, tuple) and it[0] == "SW":
                    for i in e.Switch(par_val, 2):
                        for f in (it[1] if i == 0 else it[2]):
                            f(e)
                else:
                    it(e)

        @block.sync
        def _(e):
            runner(streams["sp"], e, None)

        @block.tensor
        def _(e):
            pv = e.compute_val(e.partition_id() & 1)
            runner(streams["pe"], e, pv)

        @block.scalar
        def _(e):
            pv = e.compute_val(e.partition_id() & 1)
            runner(streams["act"], e, pv)

        @block.vector
        def _(e):
            runner(streams["dve"], e, None)

        @block.gpsimd
        def _(e):
            pv = e.compute_val(e.partition_id() & 1)
            runner(streams["pool"], e, pv)

    return nc


_NC = None


def kernel(
    L0_attn_qkv, L0_attn_Wq, L0_attn_bq, L0_attn_Wk, L0_attn_bk, L0_attn_Wv,
    L0_attn_bv, L0_attn_Wo, L0_attn_bo, L0_ffn_q_proj_w, L0_ffn_router_w,
    L0_ffn_aux_keys, L0_ffn_aux_values, L0_ffn_reliability_mask, L0_ffn_unc_w1,
    L0_ffn_unc_b1, L0_ffn_unc_w2, L0_ffn_unc_b2, L0_ffn_gate_w1, L0_ffn_gate_bias,
):
    global _NC
    f32 = np.float32
    x = np.asarray(L0_attn_qkv, f32)

    def chunks_T(v, n):
        return np.ascontiguousarray(np.asarray(v, f32).reshape(n, 128).T)

    base = {
        "wqT": np.ascontiguousarray(np.asarray(L0_attn_Wq, f32).T),
        "wkT": np.ascontiguousarray(np.asarray(L0_attn_Wk, f32).T),
        "wvT": np.ascontiguousarray(np.asarray(L0_attn_Wv, f32).T),
        "woT": np.ascontiguousarray(np.asarray(L0_attn_Wo, f32).T),
        "bqkv": np.ascontiguousarray(np.concatenate(
            [chunks_T(L0_attn_bq, 8), chunks_T(L0_attn_bk, 8),
             chunks_T(L0_attn_bv, 8)], axis=1)),
        "bo_c": chunks_T(L0_attn_bo, 8),
        "qprojT": np.ascontiguousarray(np.asarray(L0_ffn_q_proj_w, f32).T),
        "routerT": np.ascontiguousarray(np.asarray(L0_ffn_router_w, f32).T),
        "auxkT": np.ascontiguousarray(np.asarray(L0_ffn_aux_keys, f32).T),
        "auxv": np.ascontiguousarray(np.asarray(L0_ffn_aux_values, f32)),
        "rel": np.ascontiguousarray(
            np.asarray(L0_ffn_reliability_mask, f32).reshape(1, NAUX)),
        "uw1T": np.ascontiguousarray(np.asarray(L0_ffn_unc_w1, f32).T),
        "uw2c": chunks_T(np.asarray(L0_ffn_unc_w2, f32).reshape(-1), 2),
        "ub1c": chunks_T(L0_ffn_unc_b1, 2),
        "scal": np.ascontiguousarray(np.stack(
            [np.full(128, np.asarray(L0_ffn_unc_b2, f32).reshape(-1)[0], f32),
             np.full(128, f32(np.asarray(L0_ffn_gate_w1, f32)), f32),
             np.full(128, f32(np.asarray(L0_ffn_gate_bias, f32)), f32),
             np.zeros(128, f32)], axis=1)),
    }

    in_maps = []
    metas = []
    for c in range(8):
        b, par = c // 2, c % 2
        g0, g1 = CHUNKS[par]
        rows = np.r_[g0 * 1024:(g0 + 1) * 1024, g1 * 1024:(g1 + 1) * 1024]
        m = dict(base)
        m["x_full"] = np.ascontiguousarray(x[b])
        m["x_own"] = np.ascontiguousarray(x[b][rows])
        m["t_own"] = np.ascontiguousarray(rows.astype(f32).reshape(1, TOK))
        in_maps.append(m)
        metas.append((b, rows))

    if _NC is None:
        _NC = build_program()
    res = run_bass_kernel_spmd(_NC, in_maps, core_ids=list(range(8)))

    out = np.empty((B, S, D), f32)
    for c in range(8):
        b, rows = metas[c]
        out[b][rows] = res.results[c]["out"]
    return out
